# version 05
# baseline (speedup 1.0000x reference)
import math
from contextlib import ExitStack

import numpy as np

import concourse.bass as bass
import concourse.mybir as mybir
from concourse.bass_utils import run_bass_kernel_spmd

F32 = mybir.dt.float32
BF16 = mybir.dt.bfloat16
AF = mybir.ActivationFunctionType
ALU = mybir.AluOpType
AX = mybir.AxisListType

S = 4096
D = 4096
NKC = 32
NSUB = 16
TSUB = 256
NQS = 8
TQS = 512
NT = 8
EPS = 1e-5
BIG = 1.0e38
ENGS = ("sync", "gpsimd", "vector", "scalar", "tensor")

SEG_A = 0
SEG_B = 65536
SEG_C = 131072
SEG_D = 196608
STOP_AFTER = None
DBG_NSUB = NSUB
DBG_NHALF = 2
DBG_P0SUB = NSUB
DBG_BAR_ENGS = None
DBG_NOP_ENGS = None
DBG_DUMP = False
DBG_P0DUMP = False
DBG_BDUMP = False
DBG_BSTOP = None
DBG_EDUMP = False


class Prog:
    def __init__(self, nc, ctx):
        self.nc = nc
        self.ctx = ctx
        self.ops = {e: [] for e in ENGS}
        self.semh = {}
        self.semc = {}

    def sem(self, name):
        self.semh[name] = self.ctx.enter_context(self.nc.semaphore(name))
        self.semc[name] = 0

    stopped = False

    def do(self, eng, fn, inc=None, by=1):
        if self.stopped:
            return 0
        tok = None
        if inc is not None:
            self.semc[inc] += by
            tok = self.semc[inc]
            h = self.semh[inc]
            self.ops[eng].append(
                lambda e, fn=fn, h=h, by=by: fn(e).then_inc(h, by))
        else:
            self.ops[eng].append(lambda e, fn=fn: fn(e))
        return tok

    def wait(self, eng, sem, tok):
        if self.stopped or not tok or tok <= 0:
            return
        h = self.semh[sem]
        self.ops[eng].append(lambda e, h=h, tok=tok: e.wait_ge(h, tok))

    def val(self, sem):
        return self.semc[sem]

    def barrier(self):
        own = {"sync": "Sm", "gpsimd": "G", "vector": "V", "scalar": "A",
               "tensor": "T"}
        incl = DBG_BAR_ENGS if DBG_BAR_ENGS is not None else ENGS
        nope = DBG_NOP_ENGS if DBG_NOP_ENGS is not None else ENGS
        toks = {}
        for e in ENGS:
            if e not in nope:
                toks[e] = 0
            elif e == "tensor":
                toks[e] = self.do(e, lambda en: en.ldweights(
                    self.bar_marker[:]), inc=own[e])
            else:
                toks[e] = self.do(e, lambda en: en.nop(), inc=own[e])
        dma_now = {q: self.val(q) for q in ("dS", "dG", "O")}
        for e in ENGS:
            for o in ENGS:
                if o != e and o in incl:
                    self.wait(e, own[o], toks[o])
            for q in ("dS", "dG", "O"):
                self.wait(e, q, dma_now[q])


def build_nc():
    nc = bass.Bass("TRN2", target_bir_lowering=False, debug=False)
    dt = nc.dram_tensor

    xts = dt("xts", [NSUB * 128, NKC * TSUB], F32, kind="ExternalInput").ap()
    xown = dt("xown", [1024, D], F32, kind="ExternalInput").ap()
    wqh = dt("wqh", [2 * 128, NKC * 512], F32, kind="ExternalInput").ap()
    wkh = dt("wkh", [2 * 128, NKC * 512], F32, kind="ExternalInput").ap()
    wvh = dt("wvh", [2 * 128, NKC * 512], F32, kind="ExternalInput").ap()
    bqk = dt("bqk", [128, 16], F32, kind="ExternalInput").ap()
    bvrow = dt("bvrow", [1, 1024], F32, kind="ExternalInput").ap()
    wph = dt("wph", [128, 8 * D], F32, kind="ExternalInput").ap()
    bprow = dt("bprow", [1, D], F32, kind="ExternalInput").ap()
    wbth = dt("wbth", [8 * 128, NKC * 512], F32, kind="ExternalInput").ap()
    bbrow = dt("bbrow", [1, D], F32, kind="ExternalInput").ap()
    w9h = dt("w9h", [128, NKC * 9], F32, kind="ExternalInput").ap()
    apackh = dt("apackh", [128, NKC * 128], F32, kind="ExternalInput").ap()
    bpackh = dt("bpackh", [128, D], F32, kind="ExternalInput").ap()
    bthr = dt("bthr", [128, 1], F32, kind="ExternalInput").ap()
    maskpos = dt("maskpos", [128, 128], F32, kind="ExternalInput").ap()
    iotak = dt("iotak", [128, NKC], F32, kind="ExternalInput").ap()
    iotak2 = dt("iotak2", [128, NQS * NKC], F32, kind="ExternalInput").ap()
    iota512 = dt("iota512", [1, TQS], F32, kind="ExternalInput").ap()
    slopesc = dt("slopesc", [128, 8], F32, kind="ExternalInput").ap()
    slopesr = dt("slopesr", [1, 8], F32, kind="ExternalInput").ap()
    identf = dt("identf", [128, 128], F32, kind="ExternalInput").ap()

    y = dt("y", [1024, D], F32, kind="ExternalOutput").ap()
    ed = None
    ed2 = None
    if DBG_EDUMP:
        ed = dt("ed", [128, NKC * 1024], F32, kind="ExternalOutput").ap()
        ed2 = dt("ed2", [128, 4096], F32, kind="ExternalOutput").ap()
    bd = None
    if DBG_BDUMP or DBG_BSTOP is not None:
        bd = dt("bd", [128, 16384], F32, kind="ExternalOutput").ap()
    p0d = None
    if DBG_P0DUMP:
        p0d = dt("p0d", [128, 4 * NKC * TSUB + 4 * TSUB + 4 * TSUB], F32,
                 kind="ExternalOutput").ap()
    dumps = {}
    if DBG_DUMP:
        for nm, shape, dd in (("xfm", [NSUB * 128, NKC * TSUB], BF16),
                              ("qfm", [8 * 128, S], BF16),
                              ("aofm", [8 * 128, S], BF16),
                              ("pp", [S, D], F32),
                              ("ppo", [1024, D], F32),
                              ("x2h", [1024, D], F32),
                              ("ffnh", [1024, D], F32)):
            dumps[nm] = dt("dump_" + nm, shape, dd,
                           kind="ExternalOutput").ap()

    xfm = dt("xfm", [NSUB * 128, NKC * TSUB], BF16, kind="Internal").ap()
    qfm = dt("qfm", [8 * 128, S], BF16, kind="Internal").ap()
    aofm = dt("aofm", [8 * 128, S], BF16, kind="Internal").ap()
    pp = dt("pp", [S, D], F32, kind="Internal").ap()
    ppo = dt("ppo", [1024, D], F32, kind="Internal").ap()
    x2h = dt("x2h", [1024, D], F32, kind="Internal").ap()
    ffnh = dt("ffnh", [1024, D], F32, kind="Internal").ap()

    with ExitStack() as ctx:
        p = Prog(nc, ctx)
        for sname in ("dS", "dG", "T", "V", "A", "G", "Sm", "O", "CC"):
            p.sem(sname)

        seg_probe = nc.alloc_sbuf_tensor("segprobe", [128, 32],
                                         mybir.dt.uint8)
        base = nc.lookup_mloc(seg_probe).addr + 32
        base = (base + 31) // 32 * 32
        SEG_A = base
        SEG_B = base + 65536
        SEG_C = base + 131072
        SEG_D = base + 196608
        assert SEG_D + 14100 <= 229376, base

        def sb(name, shape, dtype, off):
            return nc.alloc_sbuf_tensor_at(name, shape, dtype, offset=off)

        def d_sync(fn):
            return p.do("sync", fn, inc="dS", by=16)

        def d_gp(fn):
            return p.do("gpsimd", fn, inc="dG", by=16)

        dof = [SEG_D]

        def sd(name, shape, dtype):
            nbytes = shape[1] * (2 if dtype == BF16 else 4)
            off = (dof[0] + 31) // 32 * 32
            dof[0] = off + nbytes
            return sb(name, shape, dtype, off)

        ONES128BF = sd("ones128bf", [128, 1], BF16)
        p.bar_marker = ONES128BF
        DUMV = sd("dumv", [128, 8], F32)
        DUMG = sd("dumg", [128, 8], F32)
        DUMA = sd("duma", [128, 8], F32)

        def fence(eng):
            if eng == "vector":
                return p.do("vector", lambda e: e.memset(DUMV[:], 0.0),
                            inc="V")
            if eng == "gpsimd":
                return p.do("gpsimd", lambda e: e.memset(DUMG[:], 0.0),
                            inc="G")
            return p.do("scalar", lambda e: e.activation(
                out=DUMA[:], in_=DUMA[:], func=AF.Copy), inc="A")

        def vfw():
            p.wait("vector", "V", fence("vector"))

        def afw():
            p.wait("scalar", "A", fence("scalar"))
        ONES1F = sd("ones1f", [1, 128], F32)
        EPSC = sd("epsc", [128, 1], F32)
        TINYC = sd("tinyc", [128, 1], F32)
        BQKS = sd("bqks", [128, 16], F32)
        MASKP = sd("maskp", [128, 128], F32)
        IOTAK2 = sd("iotak2s", [128, NQS * NKC], F32)
        IOTA5 = sd("iota5", [1, TQS], F32)
        SLOPC = sd("slopc", [128, 8], F32)
        SLOPR = sd("slopr", [1, 8], F32)
        BTHRS = sd("bthrs", [128, 1], F32)
        NBTHR = sd("nbthr", [128, 1], F32)
        IDENTB = sd("identb", [128, 128], BF16)
        KM2 = sd("km2", [1, 8], F32)
        LNKM = sd("lnkm", [1, 8], F32)
        LNKS = sd("lnks", [1, 8], F32)
        KMT = sd("kmt", [1, 1], F32)
        SC11 = sd("sc11", [1, 1], F32)

        QSTG = [sd(f"qstg{i}", [128, TSUB], BF16) for i in range(2)]
        m0r_off = (dof[0] + 31) // 32 * 32
        M0R = sd("m0r", [1, TSUB], F32)
        E2R = sd("e2r", [1, TSUB], F32)
        MSQR = sd("msqr", [1, TSUB], F32)
        VARR = sd("varr", [1, TSUB], F32)
        LNVR = sd("lnvr", [1, TSUB], F32)
        RSTR = sd("rstr", [1, TSUB], F32)
        BCMS = sd("bcms", [128, TSUB], F32)
        BCRS = sd("bcrs", [128, TSUB], F32)
        tmp_off = dof[0]
        TMPD = sd("tmpd", [128, TSUB], F32)
        TMPP = sd("tmpp", [128, TSUB], F32)
        LRW = sb("lrw", [1, TQS], F32, (tmp_off + 31) // 32 * 32)
        SLKHS = sb("slkhs", [128, NQS * NKC], F32, m0r_off)
        assert dof[0] <= SEG_D + 16200, dof[0]

        p.do("vector", lambda e: e.memset(ONES128BF[:], 1.0))
        p.do("vector", lambda e: e.memset(EPSC[:], EPS))
        p.do("vector", lambda e: e.memset(TINYC[:], 1e-38))
        p.do("vector", lambda e: e.memset(ONES1F[:], 1.0))
        t_ones = fence("vector")
        d_sync(lambda e: e.dma_start(out=BQKS[:], in_=bqk[:, :]))
        d_sync(lambda e: e.dma_start(out=MASKP[:], in_=maskpos[:, :]))
        d_sync(lambda e: e.dma_start(out=IOTA5[:], in_=iota512[:, :]))
        d_sync(lambda e: e.dma_start(out=SLOPC[:], in_=slopesc[:, :]))
        d_sync(lambda e: e.dma_start(out=SLOPR[:], in_=slopesr[:, :]))
        d_sync(lambda e: e.dma_start(out=IOTAK2[:], in_=iotak2[:, :]))
        t_consts = d_sync(lambda e: e.dma_start(out=BTHRS[:], in_=bthr[:, :]))
        t_cload = d_gp(lambda e: e.dma_start(out=IDENTB[:], in_=identf[:, :]))
        p.wait("vector", "dS", t_consts)
        t_nbthr = p.do("vector", lambda e: e.tensor_scalar(
            out=NBTHR[:], in0=BTHRS[:], scalar1=-1.0, scalar2=None,
            op0=ALU.mult), inc="V")
        p.wait("scalar", "V", t_ones)

        XBUF = [sb(f"xbuf{i}", [128, NKC * TSUB], F32, SEG_C + 32768 * i)
                for i in range(2)]
        XBF0 = sb("xbf0", [128, NKC * TSUB], BF16, SEG_A + 0)
        SQB0 = sb("sqb0", [128, NKC * TSUB], BF16, SEG_A + 16384)
        XFMO = [sb(f"xfmo{i}", [128, NKC * TSUB], BF16,
                   SEG_A + 32768 + 16384 * i) for i in range(2)]

        ps0 = ExitStack()
        PS_S1 = ps0.enter_context(nc.psum_tensor("ps_s1", [1, TSUB], F32))
        PS_S2 = ps0.enter_context(nc.psum_tensor("ps_s2", [1, TSUB], F32))
        PS_BM = ps0.enter_context(nc.psum_tensor("ps_bm", [128, TSUB], F32))
        PS_BR = ps0.enter_context(nc.psum_tensor("ps_br", [128, TSUB], F32))

        wr_tok = [0] * NSUB
        dve_fin = [0] * NSUB
        pool_fin = [0] * NSUB
        stats_pe_tok = 0
        stats_rows_tok = 0
        evac_prev = 0

        for sub in range(DBG_P0SUB):
            bp = sub % 2
            if sub >= 2:
                p.wait("sync", "V", dve_fin[sub - 2])
                p.wait("sync", "G", pool_fin[sub - 2])
            ld = d_sync(lambda e, sub=sub, bp=bp: e.dma_start(
                out=XBUF[bp][:], in_=xts[128 * sub:128 * (sub + 1), :]))
            p.wait("scalar", "dS", ld)
            p.wait("scalar", "T", stats_pe_tok)
            tok_cast = p.do("scalar", lambda e, bp=bp: e.activation(
                out=XBF0[:], in_=XBUF[bp][:], func=AF.Copy), inc="A")
            p.wait("vector", "A", tok_cast)
            p.wait("vector", "T", stats_pe_tok)
            tok_sq = p.do("vector", lambda e: e.tensor_tensor(
                out=SQB0[:], in0=XBF0[:], in1=XBF0[:], op=ALU.mult), inc="V")
            p.wait("tensor", "V", tok_sq)
            if sub == 0:
                p.wait("tensor", "V", t_ones)
            p.wait("tensor", "V", stats_rows_tok)
            for kc in range(NKC):
                p.do("tensor", lambda e, kc=kc, first=(kc == 0),
                     last=(kc == NKC - 1): e.matmul(
                    PS_S1[:, :], ONES128BF[:],
                    XBF0[:, TSUB * kc:TSUB * (kc + 1)],
                    start=first, stop=last))
            for kc in range(NKC):
                inc = "T" if kc == NKC - 1 else None
                tk = p.do("tensor", lambda e, kc=kc, first=(kc == 0),
                          last=(kc == NKC - 1): e.matmul(
                    PS_S2[:, :], ONES128BF[:],
                    SQB0[:, TSUB * kc:TSUB * (kc + 1)],
                    start=first, stop=last), inc=inc)
            stats_pe_tok = tk
            p.wait("vector", "T", stats_pe_tok)
            p.do("vector", lambda e: e.tensor_scalar(
                out=M0R[:], in0=PS_S1[0:1, :], scalar1=1.0 / D, scalar2=None,
                op0=ALU.mult))
            p.do("vector", lambda e: e.tensor_scalar(
                out=E2R[:], in0=PS_S2[0:1, :], scalar1=1.0 / D, scalar2=None,
                op0=ALU.mult))
            vfw()
            p.do("vector", lambda e: e.tensor_tensor(
                out=MSQR[:], in0=M0R[:], in1=M0R[:], op=ALU.mult))
            vfw()
            stats_rows_tok = p.do("vector", lambda e: e.tensor_tensor(
                out=VARR[:], in0=E2R[:], in1=MSQR[:], op=ALU.subtract),
                inc="V")
            p.wait("scalar", "V", stats_rows_tok)
            p.do("scalar", lambda e: e.activation(
                out=LNVR[:], in_=VARR[:], func=AF.Ln, bias=EPSC[0:1, 0:1],
                scale=1.0))
            afw()
            tok_rst = p.do("scalar", lambda e: e.activation(
                out=RSTR[:], in_=LNVR[:], func=AF.Exp, scale=-0.5),
                inc="A")
            p.wait("tensor", "A", tok_rst)
            p.wait("tensor", "A", evac_prev)
            p.do("tensor", lambda e: e.matmul(
                PS_BM[:, :], ONES1F[:], M0R[:], start=True, stop=True))
            tok_bc = p.do("tensor", lambda e: e.matmul(
                PS_BR[:, :], ONES1F[:], RSTR[:], start=True, stop=True),
                inc="T")
            p.wait("scalar", "T", tok_bc)
            if sub >= 1:
                p.wait("scalar", "V", dve_fin[sub - 1])
                p.wait("scalar", "G", pool_fin[sub - 1])
            p.do("scalar", lambda e: e.activation(
                out=BCMS[:], in_=PS_BM[:], func=AF.Copy))
            evac_prev = p.do("scalar", lambda e: e.activation(
                out=BCRS[:], in_=PS_BR[:], func=AF.Copy), inc="A")
            p.wait("vector", "A", evac_prev)
            p.wait("gpsimd", "A", evac_prev)
            if sub >= 2:
                p.wait("vector", "dS", wr_tok[sub - 2])
                p.wait("gpsimd", "dS", wr_tok[sub - 2])
            chunks = list(range(NKC))
            v_chunks = [kc for kc in chunks if (kc % 8) < 5]
            g_chunks = [kc for kc in chunks if (kc % 8) >= 5]
            for eng, lst, tmp in (("vector", v_chunks, TMPD),
                                  ("gpsimd", g_chunks, TMPP)):
                for n, kc in enumerate(lst):
                    p.do(eng, lambda e, kc=kc, bp=bp, tmp=tmp:
                         e.tensor_tensor(
                        out=tmp[:],
                        in0=XBUF[bp][:, TSUB * kc:TSUB * (kc + 1)],
                        in1=BCMS[:], op=ALU.subtract))
                    inc = ("V" if eng == "vector" else "G") \
                        if n == len(lst) - 1 else None
                    tk2 = p.do(eng, lambda e, kc=kc, bp=bp, tmp=tmp:
                               e.tensor_tensor(
                        out=XFMO[bp][:, TSUB * kc:TSUB * (kc + 1)],
                        in0=tmp[:], in1=BCRS[:], op=ALU.mult), inc=inc)
                    if inc == "V":
                        dve_fin[sub] = tk2
                    elif inc == "G":
                        pool_fin[sub] = tk2
                if eng == "vector":
                    dve_fin[sub] = fence("vector")
                else:
                    pool_fin[sub] = fence("gpsimd")
            p.wait("sync", "V", dve_fin[sub])
            p.wait("sync", "G", pool_fin[sub])
            wr_tok[sub] = d_sync(lambda e, sub=sub, bp=bp: e.dma_start(
                out=xfm[128 * sub:128 * (sub + 1), :], in_=XFMO[bp][:]))

        if DBG_P0DUMP:
            bpl = (DBG_P0SUB - 1) % 2
            W = NKC * TSUB
            p.wait("sync", "V", p.val("V"))
            p.wait("sync", "A", p.val("A"))
            p.wait("sync", "G", p.val("G"))
            p.wait("sync", "T", p.val("T"))
            ot = 0
            for i_, src in enumerate((XBUF[bpl], XBUF[1 - bpl])):
                ot = p.do("sync", lambda e, i_=i_, src=src, W=W: e.dma_start(
                    out=p0d[:, i_ * W:(i_ + 1) * W], in_=src[:]),
                    inc="O", by=16)
            base2 = 2 * W
            for j_, src in enumerate((BCMS, BCRS, TMPD, TMPP)):
                ot = p.do("sync", lambda e, j_=j_, src=src, base2=base2:
                          e.dma_start(
                    out=p0d[:, base2 + j_ * TSUB:base2 + (j_ + 1) * TSUB],
                    in_=src[:]), inc="O", by=16)
            base3 = base2 + 4 * TSUB
            for j_, src in enumerate((M0R, VARR, LNVR, RSTR)):
                ot = p.do("sync", lambda e, j_=j_, src=src, base3=base3:
                          e.dma_start(
                    out=p0d[0:1, base3 + j_ * TSUB:base3 + (j_ + 1) * TSUB],
                    in_=src[:]), inc="O", by=16)
            p.wait("sync", "O", ot)
            p.stopped = True
        if STOP_AFTER == "p0":
            p.stopped = True
        p.barrier()
        if STOP_AFTER == "p0b":
            p.stopped = True
        ps0.close()

        WHALF = sb("whalf", [128, NKC * 512], BF16, SEG_A + 0)
        XSTR = [sb(f"xstr{i}", [128, NKC * TSUB], BF16,
                   SEG_A + 32768 + 16384 * i) for i in range(2)]
        BVROW = sb("bvrow_s", [1, 1024], F32, SEG_B + 0)
        BVBC = sb("bvbc", [128, 1024], F32, SEG_B + 4096)
        VRES = [sb(f"vres{h}", [128, S], BF16, SEG_C + 8192 * h)
                for h in range(8)]
        KRES = [sb(f"kres{h}", [128, S], BF16, SEG_B + 8192 * h)
                for h in range(8)]

        psA = ExitStack()
        VP = [psA.enter_context(nc.psum_tensor(f"vp{i}", [128, 512], F32))
              for i in range(2)]

        tok_bv = d_sync(lambda e: e.dma_start(out=BVROW[:], in_=bvrow[:, :]))
        p.wait("tensor", "dS", tok_bv)
        p.do("tensor", lambda e: e.matmul(
            VP[0][:, :], ONES1F[:], BVROW[:, 0:512], start=True, stop=True))
        tbv = p.do("tensor", lambda e: e.matmul(
            VP[1][:, :], ONES1F[:], BVROW[:, 512:1024], start=True,
            stop=True), inc="T")
        p.wait("scalar", "T", tbv)
        p.do("scalar", lambda e: e.activation(
            out=BVBC[:, 0:512], in_=VP[0][:], func=AF.Copy))
        tok_bvbc = p.do("scalar", lambda e: e.activation(
            out=BVBC[:, 512:1024], in_=VP[1][:], func=AF.Copy), inc="A")
        p.wait("vector", "A", tok_bvbc)
        p.wait("tensor", "A", tok_bvbc)
        if STOP_AFTER == "AV0":
            p.stopped = True

        def phase_A(weights_dram, kind):
            ev_tok = {0: 0, 1: 0}
            pe_strip = {0: 0, 1: 0}
            dma_q = {0: 0, 1: 0}
            half_pe = 0
            for half in range(DBG_NHALF):
                if half == 1:
                    p.wait("gpsimd", "T", half_pe)
                wl = d_gp(lambda e, half=half: e.dma_start(
                    out=WHALF[:],
                    in_=weights_dram[128 * half:128 * (half + 1), :]))
                for sub in range(DBG_NSUB):
                    i = half * NSUB + sub
                    sp = i % 2
                    if i >= 2:
                        p.wait("sync", "T", pe_strip[sp])
                    xl = d_sync(lambda e, sub=sub, sp=sp: e.dma_start(
                        out=XSTR[sp][:],
                        in_=xfm[128 * sub:128 * (sub + 1), :]))
                    p.wait("tensor", "dS", xl)
                    p.wait("tensor", "dG", wl)
                    if kind == "V":
                        for st in range(2):
                            pe_par = st
                            p.wait("tensor", "V", ev_tok[pe_par])
                            for kc in range(NKC):
                                inc = "T" if kc == NKC - 1 else None
                                tk = p.do("tensor", lambda e, kc=kc, sp=sp,
                                          st=st, pe_par=pe_par,
                                          first=(kc == 0),
                                          last=(kc == NKC - 1): e.matmul(
                                    VP[pe_par][:, :],
                                    XSTR[sp][:, TSUB * kc + 128 * st:
                                             TSUB * kc + 128 * (st + 1)],
                                    WHALF[:, 512 * kc:512 * (kc + 1)],
                                    start=first, stop=last), inc=inc)
                            pe_strip[sp] = tk
                            half_pe = tk
                            p.wait("vector", "T", tk)
                            last_v = 0
                            for hr in range(4):
                                h = 4 * half + hr
                                col = 128 * (2 * sub + st)
                                inc = "V" if hr == 3 else None
                                last_v = p.do("vector", lambda e, h=h, hr=hr,
                                              col=col, pe_par=pe_par:
                                              e.tensor_tensor(
                                    out=VRES[h][:, col:col + 128],
                                    in0=VP[pe_par][:, 128 * hr:
                                                   128 * (hr + 1)],
                                    in1=BVBC[:, 128 * h:128 * (h + 1)],
                                    op=ALU.add), inc=inc)
                            ev_tok[pe_par] = last_v
                    else:
                        for hc in range(4):
                            h = 4 * half + hc
                            pe_par = hc % 2
                            p.wait("tensor", "V", ev_tok[pe_par])
                            for kc in range(NKC):
                                inc = "T" if kc == NKC - 1 else None
                                tk = p.do("tensor", lambda e, kc=kc, sp=sp,
                                          hc=hc, pe_par=pe_par,
                                          first=(kc == 0),
                                          last=(kc == NKC - 1): e.matmul(
                                    VP[pe_par][:, 0:TSUB],
                                    WHALF[:, 512 * kc + 128 * hc:
                                           512 * kc + 128 * (hc + 1)],
                                    XSTR[sp][:, TSUB * kc:TSUB * (kc + 1)],
                                    start=first, stop=last), inc=inc)
                            pe_strip[sp] = tk
                            half_pe = tk
                            p.wait("vector", "T", tk)
                            if kind == "K":
                                ev_tok[pe_par] = p.do(
                                    "vector",
                                    lambda e, h=h, sub=sub, pe_par=pe_par:
                                    e.tensor_scalar(
                                        out=KRES[h][:, TSUB * sub:
                                                    TSUB * (sub + 1)],
                                        in0=VP[pe_par][:, 0:TSUB],
                                        scalar1=BQKS[:, 8 + h:9 + h],
                                        scalar2=None, op0=ALU.add), inc="V")
                            else:
                                qp = (i * 4 + hc) % 2
                                p.wait("vector", "dS", dma_q[qp])
                                p.do(
                                    "vector",
                                    lambda e, h=h, qp=qp, pe_par=pe_par:
                                    e.tensor_scalar(
                                        out=QSTG[qp][:],
                                        in0=VP[pe_par][:, 0:TSUB],
                                        scalar1=BQKS[:, h:h + 1],
                                        scalar2=1.0 / 128.0,
                                        op0=ALU.add, op1=ALU.mult))
                                ev_tok[pe_par] = fence("vector")
                                p.wait("sync", "V", ev_tok[pe_par])
                                dma_q[qp] = d_sync(
                                    lambda e, h=h, sub=sub, qp=qp:
                                    e.dma_start(
                                        out=qfm[128 * h:128 * (h + 1),
                                                TSUB * sub:TSUB * (sub + 1)],
                                        in_=QSTG[qp][:]))

        phase_A(wvh, "V")
        if STOP_AFTER == "AV":
            p.stopped = True
        p.barrier()
        phase_A(wkh, "K")
        if STOP_AFTER == "AK":
            p.stopped = True
        p.barrier()
        phase_A(wqh, "Q")
        if STOP_AFTER == "A":
            p.stopped = True
        p.barrier()
        psA.close()

        QBUF = [sb(f"qbuf{i}", [128, S], BF16, SEG_A + 8192 * i)
                for i in range(2)]
        T1B = [sb(f"t1b{i}", [128, TQS], F32, SEG_A + 16384 + 2048 * i)
               for i in range(2)]
        PTB = [sb(f"ptb{i}", [128, TQS], BF16, SEG_A + 20480 + 1024 * i)
               for i in range(3)]
        DBCA = [sb(f"dbca{s}", [128, TQS], F32, SEG_A + 23552 + 2048 * s)
                for s in range(NQS)]
        LBCB = [sb(f"lbcb{i}", [128, TQS], F32, SEG_A + 39936 + 2048 * i)
                for i in range(2)]
        AOEV = [sb(f"aoev{i}", [128, TQS], BF16, SEG_A + 44032 + 1024 * i)
                for i in range(2)]
        KSQB = [sb(f"ksqb{i}", [128, TQS], BF16, SEG_A + 46080 + 1024 * i)
                for i in range(2)]
        QSQB = [sb(f"qsqb{i}", [128, TQS], BF16, SEG_A + 48128 + 1024 * i)
                for i in range(2)]
        DLN = [sb(f"dln{i}", [1, TQS], F32, SEG_A + 50176 + 2048 * i)
               for i in range(2)]
        CSR = [sb(f"csr{i}", [1, TQS], F32, SEG_A + 54272 + 2048 * i)
               for i in range(2)]
        DRW = [sb(f"drw{i}", [1, TQS], F32, SEG_A + 58368 + 2048 * i)
               for i in range(2)]
        LIV = sb("liv", [1, TQS], F32, SEG_A + 62464)

        psB = ExitStack()
        SPS = [psB.enter_context(nc.psum_tensor(f"sps{i}", [128, TQS], F32))
               for i in range(3)]
        AVPS = psB.enter_context(nc.psum_tensor("avps", [128, TQS], F32))
        LPS = psB.enter_context(nc.psum_tensor("lps", [128, TQS], F32))
        AX1 = psB.enter_context(nc.psum_tensor("ax1", [128, TQS], F32))
        AX2 = psB.enter_context(nc.psum_tensor("ax2", [128, TQS], F32))
        AX3 = psB.enter_context(nc.psum_tensor("ax3", [128, TQS], F32))

        k2_tok = {0: 0, 1: 0}
        red_tok = 0
        for h in range(8):
            for s8 in range(NQS):
                par = (h * NQS + s8) % 2
                p.wait("vector", "T", k2_tok[par])
                kq_tok = p.do("vector", lambda e, h=h, s8=s8, par=par:
                              e.tensor_tensor(
                    out=KSQB[par][:],
                    in0=KRES[h][:, TQS * s8:TQS * (s8 + 1)],
                    in1=KRES[h][:, TQS * s8:TQS * (s8 + 1)], op=ALU.mult),
                    inc="V")
                p.wait("tensor", "V", kq_tok)
                p.wait("tensor", "V", red_tok)
                k2_tok[par] = p.do("tensor", lambda e, par=par: e.matmul(
                    (AX1 if par == 0 else AX2)[0:1, :], ONES128BF[:],
                    KSQB[par][:], start=True, stop=True), inc="T")
                p.wait("vector", "T", k2_tok[par])
                p.do("vector", lambda e, par=par: e.reduce_max(
                    out=KMT[:], in_=(AX1 if par == 0 else AX2)[0:1, :],
                    axis=AX.X))
                vfw()
                if s8 == 0:
                    red_tok = p.do("vector", lambda e, h=h: e.tensor_copy(
                        out=KM2[0:1, h:h + 1], in_=KMT[:]), inc="V")
                else:
                    red_tok = p.do("vector", lambda e, h=h: e.tensor_tensor(
                        out=KM2[0:1, h:h + 1], in0=KM2[0:1, h:h + 1],
                        in1=KMT[:], op=ALU.max), inc="V")
        p.wait("scalar", "V", red_tok)
        p.do("scalar", lambda e: e.activation(
            out=LNKM[:], in_=KM2[:], func=AF.Ln, bias=TINYC[0:1, 0:1],
            scale=1.0))
        p.do("scalar", lambda e: e.activation(
            out=LNKS[:], in_=LNKM[:], func=AF.Copy, scale=0.5))
        tok_lnk = fence("scalar")
        p.barrier()

        ql = [0] * 9
        q_done = [0] * 8
        dbc_used = {}
        aoev_dma = {0: 0, 1: 0}
        lbc_evac = {0: 0, 1: 0}
        d2_act = 0
        dbc_ev_prev = 0
        dbc_ev_tok = [0] * NQS
        qsq_free = {0: 0, 1: 0}
        av_evac_tok = 0
        lrow_tok = 0

        ql[0] = d_sync(lambda e: e.dma_start(
            out=QBUF[0][:], in_=qfm[0:128, :]))

        for h in range(8):
            hp = h % 2
            if h + 1 < 8:
                if h >= 1:
                    p.wait("sync", "T", q_done[h - 1])
                ql[h + 1] = d_sync(lambda e, h=h: e.dma_start(
                    out=QBUF[(h + 1) % 2][:],
                    in_=qfm[128 * (h + 1):128 * (h + 2), :]))
            p.do("vector", lambda e, h=h: e.tensor_scalar(
                out=SLKHS[:], in0=IOTAK2[:], scalar1=SLOPC[:, h:h + 1],
                scalar2=None, op0=ALU.mult))
            tok_slk = fence("vector")
            p.wait("vector", "dS", ql[h])
            for s in range(NQS):
                par = s % 2
                p.wait("vector", "T", qsq_free[par])
                tok_qsq = p.do("vector", lambda e, s=s, par=par, hp=hp:
                               e.tensor_tensor(
                    out=QSQB[par][:],
                    in0=QBUF[hp][:, TQS * s:TQS * (s + 1)],
                    in1=QBUF[hp][:, TQS * s:TQS * (s + 1)], op=ALU.mult),
                    inc="V")
                p.wait("tensor", "V", tok_qsq)
                p.wait("tensor", "A", d2_act)
                qsq_free[par] = p.do("tensor", lambda e, par=par: e.matmul(
                    AX1[0:1, :], ONES128BF[:], QSQB[par][:],
                    start=True, stop=True), inc="T")
                p.wait("scalar", "T", qsq_free[par])
                p.do("scalar", lambda e, par=par: e.activation(
                    out=DLN[par][:], in_=AX1[0:1, :], func=AF.Ln,
                    bias=TINYC[0:1, 0:1], scale=1.0))
                afw()
                d2_act = p.do("scalar", lambda e, h=h, par=par: e.activation(
                    out=CSR[par][:], in_=DLN[par][:], func=AF.Exp,
                    bias=LNKS[0:1, h:h + 1], scale=0.5), inc="A")
                p.wait("vector", "A", d2_act)
                tok_drw = p.do("vector", lambda e, h=h, par=par:
                               e.scalar_tensor_tensor(
                    out=DRW[par][:], in0=IOTA5[:],
                    scalar=SLOPR[0:1, h:h + 1], in1=CSR[par][:],
                    op0=ALU.mult, op1=ALU.add), inc="V")
                p.wait("tensor", "V", tok_drw)
                p.wait("tensor", "A", dbc_ev_prev)
                p.wait("tensor", "V", dbc_used.get((h - 1, s), 0))
                dbc_mm = p.do("tensor", lambda e, par=par: e.matmul(
                    AX2[:, :], ONES1F[:], DRW[par][:], start=True,
                    stop=True), inc="T")
                p.wait("scalar", "T", dbc_mm)
                dbc_ev_prev = p.do("scalar", lambda e, s=s: e.activation(
                    out=DBCA[s][:], in_=AX2[:], func=AF.Copy), inc="A")
                dbc_ev_tok[s] = dbc_ev_prev
            for s in range(NQS):
                nkc = 4 * s + 4
                stt_tok = {}
                exp_tok = {}
                lmm_tok = {}
                p.wait("vector", "A", dbc_ev_tok[s])
                p.wait("vector", "V", tok_slk)
                for kc in range(nkc):
                    jst = 0 if kc < 4 * s else 128 * (kc - 4 * s)
                    w = TQS - jst
                    sp = kc % 3
                    t1p = kc % 2
                    slot = kc % 3
                    if kc >= 3:
                        p.wait("tensor", "V", stt_tok[kc - 3])
                    if kc == 0:
                        p.wait("tensor", "dS", ql[h])
                    smm = p.do("tensor", lambda e, h=h, s=s, kc=kc, sp=sp,
                               jst=jst, hp=hp: e.matmul(
                        SPS[sp][:, jst:TQS],
                        KRES[h][:, 128 * kc:128 * (kc + 1)],
                        QBUF[hp][:, TQS * s + jst:TQS * (s + 1)],
                        start=True, stop=True), inc="T")
                    p.wait("vector", "T", smm)
                    if kc >= 2:
                        p.wait("vector", "A", exp_tok[kc - 2])
                    stt_tok[kc] = p.do("vector", lambda e, kc=kc, sp=sp,
                                       t1p=t1p, jst=jst, w=w, s=s:
                                       e.tensor_tensor(
                        out=T1B[t1p][:, 0:w], in0=SPS[sp][:, jst:TQS],
                        in1=DBCA[s][:, jst:TQS], op=ALU.subtract), inc="V")
                    if DBG_BSTOP == (h, s, kc):
                        p.wait("scalar", "V", stt_tok[kc])
                        p.do("scalar", lambda e, sp=sp: e.activation(
                            out=T1B[1][:], in_=SPS[sp][:], func=AF.Copy))
                        dsem = p.do("scalar", lambda e: e.activation(
                            out=DUMA[:], in_=DUMA[:], func=AF.Copy),
                            inc="A")
                        p.wait("sync", "A", dsem)
                        col = [0]

                        def bdump2(src, n, rows=128):
                            c0 = col[0]
                            col[0] += n
                            p.do("sync", lambda e, src=src, c0=c0, n=n,
                                 rows=rows: e.dma_start(
                                out=bd[0:rows, c0:c0 + n], in_=src[:],
                                allow_slow_non_contiguous=True),
                                inc="O", by=16)
                        bdump2(T1B[0], TQS)
                        bdump2(T1B[1], TQS)
                        bdump2(DBCA[s], TQS)
                        bdump2(SLKHS, NQS * NKC)
                        ot2 = p.do("sync", lambda e: e.dma_start(
                            out=bd[0:1, 8000:8008], in_=KM2[:]),
                            inc="O", by=16)
                        p.wait("sync", "O", ot2)
                        p.stopped = True
                    if kc >= 4 * s:
                        stt_tok[kc] = p.do("vector", lambda e, t1p=t1p:
                                           e.tensor_tensor(
                            out=T1B[t1p][:, 0:128], in0=T1B[t1p][:, 0:128],
                            in1=MASKP[:], op=ALU.subtract), inc="V")
                    p.wait("scalar", "V", stt_tok[kc])
                    if kc >= 3:
                        p.wait("scalar", "T", lmm_tok[kc - 3])
                    exp_tok[kc] = p.do("scalar", lambda e, kc=kc, t1p=t1p,
                                       slot=slot, w=w, s=s: e.activation(
                        out=PTB[slot][:, 0:w], in_=T1B[t1p][:, 0:w],
                        func=AF.Exp,
                        bias=SLKHS[:, NKC * s + kc:NKC * s + kc + 1]),
                        inc="A")
                    p.wait("tensor", "A", exp_tok[kc])
                    if kc == 0 and (h, s) != (0, 0):
                        p.wait("tensor", "V", av_evac_tok)
                        p.wait("tensor", "V", lrow_tok)
                    p.do("tensor", lambda e, h=h, kc=kc, slot=slot, jst=jst,
                         w=w, first=(kc == 0), last=(kc == nkc - 1):
                         e.matmul(
                        AVPS[:, jst:TQS],
                        VRES[h][:, 128 * kc:128 * (kc + 1)],
                        PTB[slot][:, 0:w], start=first, stop=last))
                    lmm_tok[kc] = p.do("tensor", lambda e, kc=kc, slot=slot,
                                       jst=jst, w=w, first=(kc == 0),
                                       last=(kc == nkc - 1): e.matmul(
                        LPS[0:1, jst:TQS], ONES128BF[:], PTB[slot][:, 0:w],
                        start=first, stop=last), inc="T")
                q_done[h] = lmm_tok[nkc - 1]
                dbc_used[(h, s)] = stt_tok[nkc - 1]
                par = s % 2
                p.wait("vector", "T", lmm_tok[nkc - 1])
                p.do("vector", lambda e: e.tensor_scalar(
                    out=LRW[:], in0=LPS[0:1, :], scalar1=1e-30, scalar2=None,
                    op0=ALU.max))
                vfw()
                lrow_tok = p.do("vector", lambda e: e.reciprocal(
                    out=LIV[:], in_=LRW[:]), inc="V")
                p.wait("tensor", "V", lrow_tok)
                p.wait("tensor", "A", max(lbc_evac[0], lbc_evac[1]))
                lbc_mm = p.do("tensor", lambda e: e.matmul(
                    AX3[:, :], ONES1F[:], LIV[:], start=True, stop=True),
                    inc="T")
                p.wait("scalar", "T", lbc_mm)
                lbc_evac[par] = p.do("scalar", lambda e, par=par:
                                     e.activation(
                    out=LBCB[par][:], in_=AX3[:], func=AF.Copy), inc="A")
                p.wait("vector", "A", lbc_evac[par])
                p.wait("vector", "dS", aoev_dma[par])
                p.do("vector", lambda e, par=par: e.tensor_tensor(
                    out=AOEV[par][:], in0=AVPS[:], in1=LBCB[par][:],
                    op=ALU.mult))
                av_evac_tok = fence("vector")
                p.wait("sync", "V", av_evac_tok)
                aoev_dma[par] = d_sync(lambda e, h=h, s=s, par=par:
                                       e.dma_start(
                    out=aofm[128 * h:128 * (h + 1), TQS * s:TQS * (s + 1)],
                    in_=AOEV[par][:]))

        if DBG_BDUMP:
            for q in ("dS", "dG"):
                p.wait("sync", q, p.val(q))
            for sem in ("V", "A", "G", "T"):
                p.wait("sync", sem, p.val(sem))
            col = [0]

            def bdump(src, n):
                c0 = col[0]
                col[0] += n
                rows = 128 if src.shape[0] == 128 else 1
                p.do("sync", lambda e, src=src, c0=c0, n=n, rows=rows:
                     e.dma_start(out=bd[0:rows, c0:c0 + n], in_=src[:],
                                 allow_slow_non_contiguous=True),
                     inc="O", by=16)
            for s_ in range(NQS):
                bdump(DBCA[s_], TQS)
            bdump(SLKH, NKC)
            bdump(SLKHS, NKC)
            bdump(SCCOL, 1)
            for i_ in range(2):
                bdump(CSR[i_], TQS)
            for i_ in range(2):
                bdump(DRW[i_], TQS)
            for i_ in range(2):
                bdump(DLN[i_], TQS)
            bdump(LIV, TQS)
            for i_ in range(2):
                bdump(LBCB[i_], TQS)
            for i_ in range(2):
                bdump(T1B[i_], TQS)
            ot2 = p.do("sync", lambda e: e.dma_start(
                out=bd[0:1, 16000:16008], in_=KM2[:]), inc="O", by=16)
            p.wait("sync", "O", ot2)
            p.stopped = True
        if STOP_AFTER == "B":
            p.stopped = True
        p.barrier()
        psB.close()

        WPB = sb("wpb", [128, 8 * D], BF16, SEG_A + 0)
        AOS = [sb(f"aos{h}", [128, S], BF16, SEG_B + 8192 * h)
               for h in range(8)]
        PPB = [sb(f"ppb{i}", [128, 512], F32, SEG_C + 2048 * i)
               for i in range(3)]

        psC = ExitStack()
        PPS = [psC.enter_context(nc.psum_tensor(f"pps{i}", [128, 512], F32))
               for i in range(3)]

        wp_tok = d_gp(lambda e: e.dma_start(out=WPB[:], in_=wph[:, :]))
        ao_tok = 0
        for h in range(8):
            ao_tok = d_sync(lambda e, h=h: e.dma_start(
                out=AOS[h][:], in_=aofm[128 * h:128 * (h + 1), :]))
        p.wait("tensor", "dG", wp_tok)
        p.wait("tensor", "dS", ao_tok)

        pp_wr = 0
        evac_c = {0: 0, 1: 0, 2: 0}
        dma_c = {0: 0, 1: 0, 2: 0}
        for t in range(32):
            for dm in range(8):
                i = t * 8 + dm
                sl = i % 3
                p.wait("tensor", "A", evac_c[sl])
                for h in range(8):
                    inc = "T" if h == 7 else None
                    tk = p.do("tensor", lambda e, t=t, dm=dm, h=h, sl=sl,
                              first=(h == 0), last=(h == 7): e.matmul(
                        PPS[sl][:, :], AOS[h][:, 128 * t:128 * (t + 1)],
                        WPB[:, D * h + 512 * dm:D * h + 512 * (dm + 1)],
                        start=first, stop=last), inc=inc)
                p.wait("scalar", "T", tk)
                p.wait("scalar", "dS", dma_c[sl])
                p.do("scalar", lambda e, sl=sl: e.activation(
                    out=PPB[sl][:], in_=PPS[sl][:], func=AF.Copy))
                evac_c[sl] = fence("scalar")
                p.wait("sync", "A", evac_c[sl])
                dma_c[sl] = d_sync(lambda e, t=t, dm=dm, sl=sl: e.dma_start(
                    out=pp[128 * t:128 * (t + 1), 512 * dm:512 * (dm + 1)],
                    in_=PPB[sl][:]))
                pp_wr = dma_c[sl]
        if STOP_AFTER == "C":
            p.stopped = True
        psC.close()

        p.wait("gpsimd", "dS", pp_wr)
        cc_tok = p.do("gpsimd", lambda e: e.collective_compute(
            "ReduceScatter", ALU.add,
            replica_groups=[[0, 1, 2, 3], [4, 5, 6, 7]],
            ins=[pp.opt()], outs=[ppo.opt()]), inc="CC")

        if STOP_AFTER == "D":
            p.stopped = True
        p.barrier()

        BPBC = sb("bpbc", [128, D], F32, SEG_C + 0)
        BBBC = sb("bbbc", [128, D], F32, SEG_C + 16384)
        APB = sb("apb", [128, NKC * 128], BF16, SEG_C + 32768)
        BPKB = sb("bpkb", [128, D], BF16, SEG_C + 40960)
        W9B = sb("w9b", [128, NKC * 9], BF16, SEG_C + 49152)
        HWT = [sb(f"hwt{t}", [128, 128], BF16, SEG_C + 49760 + 256 * t)
               for t in range(NT)]
        FFNE = [sb(f"ffne{i}", [128, 512], F32, SEG_C + 51840 + 2048 * i)
                for i in range(3)]
        cof = [SEG_C + 58000]

        def sc(name, shape, dtype):
            nbytes = shape[1] * (2 if dtype == BF16 else 4)
            off = (cof[0] + 31) // 32 * 32
            cof[0] = off + nbytes
            return sb(name, shape, dtype, off)

        BNS = sc("bns", [128, 48], F32)
        MV = sc("mv", [128, 2], F32)
        MCOL = sc("mcol", [128, 1], F32)
        RCOL = sc("rcol", [128, 1], F32)
        GEX = sc("gex", [128, 8], F32)
        GMX = sc("gmx", [128, 1], F32)
        GSM = sc("gsm", [128, 1], F32)
        GIV = sc("giv", [128, 1], F32)
        THR = sc("thr", [128, 1], F32)
        ADP = sc("adp", [128, 8], F32)
        WSM = sc("wsm", [128, 1], F32)
        IZC = sc("izc", [128, 1], F32)
        WIV = sc("wiv", [128, 1], F32)
        WFN = sc("wfn", [128, 8], F32)
        GM2 = sc("gm2", [128, 1], F32)
        THS = sc("ths", [128, 1], F32)
        AD2 = sc("ad2", [128, 8], F32)
        HWB = sc("hwb", [128, 128], BF16)
        assert cof[0] <= SEG_C + 65536

        BPR = sb("bpr", [1, D], F32, SEG_A + 0)
        BBR = sb("bbr", [1, D], F32, SEG_A + 16384)
        YTB = sb("ytb", [128, D], F32, SEG_A + 0)
        XTB = sb("xtb", [128, D], F32, SEG_A + 16384)
        X2B = sb("x2b", [128, D], F32, SEG_A + 32768)
        XFT = sb("xft", [128, D], BF16, SEG_A + 49152)
        XFFM = sb("xffm", [128, NKC * 1024], BF16, SEG_B + 0)
        WBT = [sb(f"wbt{i}", [128, NKC * 512], BF16, SEG_A + 32768 * i)
               for i in range(2)]

        psE = ExitStack()
        TRPS = [psE.enter_context(nc.psum_tensor(f"trps{i}", [128, 128],
                                                 BF16)) for i in range(2)]
        GPS = psE.enter_context(nc.psum_tensor("gps", [128, 9], F32))
        HPS = psE.enter_context(nc.psum_tensor("hps", [128, 128], F32))
        BPS = [psE.enter_context(nc.psum_tensor(f"bps{i}", [128, 512], F32))
               for i in range(3)]
        BCP = psE.enter_context(nc.psum_tensor("bcp", [128, 512], F32))

        tok_e1 = d_gp(lambda e: e.dma_start(out=APB[:], in_=apackh[:, :]))
        tok_e2 = d_gp(lambda e: e.dma_start(out=BPKB[:], in_=bpackh[:, :]))
        tok_e3 = d_gp(lambda e: e.dma_start(out=W9B[:], in_=w9h[:, :]))
        d_sync(lambda e: e.dma_start(out=BPR[:], in_=bprow[:, :]))
        tok_e5 = d_sync(lambda e: e.dma_start(out=BBR[:], in_=bbrow[:, :]))
        p.wait("tensor", "dS", tok_e5)
        bc_ev = 0
        for i in range(16):
            row = BPR if i < 8 else BBR
            dst = BPBC if i < 8 else BBBC
            j = i % 8
            p.wait("tensor", "A", bc_ev)
            tkm = p.do("tensor", lambda e, row=row, j=j: e.matmul(
                BCP[:, :], ONES1F[:], row[0:1, 512 * j:512 * (j + 1)],
                start=True, stop=True), inc="T")
            p.wait("scalar", "T", tkm)
            bc_ev = p.do("scalar", lambda e, dst=dst, j=j: e.activation(
                out=dst[:, 512 * j:512 * (j + 1)], in_=BCP[:], func=AF.Copy),
                inc="A")

        x2_wr = [0] * NT
        hwt_tok = [0] * NT
        trev = {0: 0, 1: 0}
        gps_free = 0
        hps_free = 0
        hwb_free = 0
        xft_free = 0
        pre_v_last = 0
        tok_x2_prev = 0
        x2_wr_prev = 0
        for t in range(NT):
            p.wait("sync", "CC", cc_tok)
            p.wait("sync", "A", bc_ev)
            p.wait("sync", "V", tok_x2_prev)
            d_sync(lambda e, t=t: e.dma_start(
                out=YTB[:], in_=ppo[128 * t:128 * (t + 1), :]))
            lt2 = d_sync(lambda e, t=t: e.dma_start(
                out=XTB[:], in_=xown[128 * t:128 * (t + 1), :]))
            p.wait("vector", "dS", lt2)
            p.wait("vector", "A", bc_ev)
            p.wait("vector", "dS", x2_wr_prev)
            p.do("vector", lambda e: e.tensor_tensor(
                out=YTB[:], in0=YTB[:], in1=BPBC[:], op=ALU.add))
            for i in range(8):
                p.do("vector", lambda e, i=i: e.bn_stats(
                    out=BNS[:, 6 * i:6 * (i + 1)],
                    in_=YTB[:, 512 * i:512 * (i + 1)]))
            p.do("vector", lambda e: e.bn_aggr(
                out=MV[:], in_=BNS[:]))
            tok_mv = fence("vector")
            p.wait("scalar", "V", tok_mv)
            p.do("scalar", lambda e: e.activation(
                out=MCOL[:], in_=MV[:, 1:2], func=AF.Ln, bias=EPSC[:],
                scale=1.0))
            afw()
            p.do("scalar", lambda e: e.activation(
                out=RCOL[:], in_=MCOL[:], func=AF.Exp, scale=-0.5))
            tok_r2 = fence("scalar")
            p.wait("vector", "A", tok_r2)
            p.wait("vector", "V", tok_mv)
            p.do("vector", lambda e: e.tensor_scalar(
                out=X2B[:], in0=YTB[:], scalar1=MV[:, 0:1], scalar2=RCOL[:],
                op0=ALU.subtract, op1=ALU.mult))
            p.do("vector", lambda e: e.tensor_tensor(
                out=X2B[:], in0=X2B[:], in1=XTB[:], op=ALU.add))
            tok_x2 = fence("vector")
            tok_x2_prev = tok_x2
            p.wait("sync", "V", tok_x2)
            x2_wr[t] = d_sync(lambda e, t=t: e.dma_start(
                out=x2h[128 * t:128 * (t + 1), :], in_=X2B[:]))
            x2_wr_prev = x2_wr[t]
            for i in range(8):
                p.do("vector", lambda e, i=i: e.bn_stats(
                    out=BNS[:, 6 * i:6 * (i + 1)],
                    in_=X2B[:, 512 * i:512 * (i + 1)]))
            p.do("vector", lambda e: e.bn_aggr(
                out=MV[:], in_=BNS[:]))
            tok_mv3 = fence("vector")
            p.wait("scalar", "V", tok_mv3)
            p.do("scalar", lambda e: e.activation(
                out=MCOL[:], in_=MV[:, 1:2], func=AF.Ln, bias=EPSC[:],
                scale=1.0))
            afw()
            p.do("scalar", lambda e: e.activation(
                out=RCOL[:], in_=MCOL[:], func=AF.Exp, scale=-0.5))
            tok_r3 = fence("scalar")
            p.wait("vector", "A", tok_r3)
            p.wait("vector", "V", tok_mv3)
            p.wait("vector", "A", xft_free)
            p.do("vector", lambda e: e.tensor_scalar(
                out=XFT[:], in0=X2B[:], scalar1=MV[:, 0:1], scalar2=RCOL[:],
                op0=ALU.subtract, op1=ALU.mult))
            tok_xf = fence("vector")
            p.wait("tensor", "V", tok_xf)
            if t == 0:
                p.wait("tensor", "dG", t_cload)
            for kc in range(NKC):
                tp = kc % 2
                p.wait("tensor", "A", trev[tp])
                tk = p.do("tensor", lambda e, kc=kc, tp=tp: e.transpose(
                    TRPS[tp][:, :], XFT[:, 128 * kc:128 * (kc + 1)],
                    IDENTB[:]), inc="T")
                p.wait("scalar", "T", tk)
                trev[tp] = p.do("scalar", lambda e, kc=kc, t=t, tp=tp:
                                e.activation(
                    out=XFFM[:, 1024 * kc + 128 * t:
                             1024 * kc + 128 * (t + 1)],
                    in_=TRPS[tp][:], func=AF.Copy), inc="A")
            xffm_tok = fence("scalar")
            xft_free = xffm_tok
            p.wait("tensor", "A", xffm_tok)
            if t == 0:
                p.wait("tensor", "dG", tok_e3)
                p.wait("tensor", "dG", tok_e1)
            p.wait("tensor", "A", gps_free)
            p.wait("tensor", "V", hps_free)
            for kc in range(NKC):
                p.do("tensor", lambda e, kc=kc, t=t, first=(kc == 0),
                     last=(kc == NKC - 1): e.matmul(
                    GPS[:, :],
                    XFFM[:, 1024 * kc + 128 * t:1024 * kc + 128 * (t + 1)],
                    W9B[:, 9 * kc:9 * (kc + 1)], start=first, stop=last))
            tok_h = 0
            for kc in range(NKC):
                inc = "T" if kc == NKC - 1 else None
                tok_h = p.do("tensor", lambda e, kc=kc, t=t, first=(kc == 0),
                             last=(kc == NKC - 1): e.matmul(
                    HPS[:, :],
                    XFFM[:, 1024 * kc + 128 * t:1024 * kc + 128 * (t + 1)],
                    APB[:, 128 * kc:128 * (kc + 1)], start=first, stop=last),
                    inc=inc)
            p.wait("vector", "T", tok_h)
            if t == 0:
                p.wait("vector", "V", t_nbthr)
            p.do("vector", lambda e: e.reduce_max(
                out=GMX[:], in_=GPS[:, 0:8], axis=AX.X))
            p.do("vector", lambda e: e.tensor_scalar(
                out=GM2[:], in0=GMX[:], scalar1=-1.0, scalar2=None,
                op0=ALU.mult))
            tok_gm = fence("vector")
            p.wait("scalar", "V", tok_gm)
            tok_ge = p.do("scalar", lambda e: e.activation(
                out=GEX[:], in_=GPS[:, 0:8], func=AF.Exp, bias=GM2[:],
                scale=1.0), inc="A")
            gps_free = p.do("scalar", lambda e: e.activation(
                out=THR[:], in_=GPS[:, 8:9], func=AF.Exp, bias=NBTHR[:],
                scale=-1.0), inc="A")
            p.wait("vector", "A", gps_free)
            p.do("vector", lambda e: e.reduce_sum(
                out=GSM[:], in_=GEX[:], axis=AX.X))
            vfw()
            p.do("vector", lambda e: e.reciprocal(out=GIV[:], in_=GSM[:]))
            p.do("vector", lambda e: e.tensor_scalar(
                out=THS[:], in0=THR[:], scalar1=1.0, scalar2=None,
                op0=ALU.add))
            vfw()
            p.do("vector", lambda e: e.reciprocal(out=IZC[:], in_=THS[:]))
            vfw()
            p.do("vector", lambda e: e.tensor_scalar(
                out=THR[:], in0=IZC[:], scalar1=0.125, scalar2=None,
                op0=ALU.mult))
            vfw()
            p.do("vector", lambda e: e.tensor_scalar(
                out=ADP[:], in0=GEX[:], scalar1=GIV[:], scalar2=THR[:],
                op0=ALU.mult, op1=ALU.subtract))
            vfw()
            p.do("vector", lambda e: e.tensor_scalar(
                out=AD2[:], in0=ADP[:], scalar1=0.0, scalar2=None,
                op0=ALU.max))
            vfw()
            p.do("vector", lambda e: e.reduce_sum(
                out=WSM[:], in_=AD2[:], axis=AX.X))
            vfw()
            p.do("vector", lambda e: e.tensor_single_scalar(
                out=IZC[:], in_=WSM[:], scalar=0.0, op=ALU.is_equal))
            vfw()
            p.do("vector", lambda e: e.tensor_tensor(
                out=WSM[:], in0=WSM[:], in1=IZC[:], op=ALU.add))
            vfw()
            p.do("vector", lambda e: e.reciprocal(out=WIV[:], in_=WSM[:]))
            vfw()
            p.do("vector", lambda e: e.tensor_scalar(
                out=WFN[:], in0=AD2[:], scalar1=WIV[:], scalar2=2.0,
                op0=ALU.mult, op1=ALU.mult))
            tok_hw = 0
            p.wait("vector", "V", fence("vector"))
            p.wait("vector", "T", hwb_free)
            for ei in range(8):
                p.do("vector", lambda e, ei=ei: e.tensor_scalar(
                    out=HWB[:, 16 * ei:16 * (ei + 1)],
                    in0=HPS[:, 16 * ei:16 * (ei + 1)],
                    scalar1=WFN[:, ei:ei + 1], scalar2=None, op0=ALU.mult))
            tok_hw = fence("vector")
            hps_free = tok_hw
            pre_v_last = tok_hw
            p.wait("tensor", "V", tok_hw)
            tp = t % 2
            p.wait("tensor", "A", trev[tp])
            tk = p.do("tensor", lambda e, tp=tp: e.transpose(
                TRPS[tp][:, :], HWB[:], IDENTB[:]), inc="T")
            hwb_free = tk
            p.wait("scalar", "T", tk)
            trev[tp] = p.do("scalar", lambda e, t=t, tp=tp: e.activation(
                out=HWT[t][:], in_=TRPS[tp][:], func=AF.Copy), inc="A")
            hwt_tok[t] = fence("scalar")

        if DBG_EDUMP:
            for q in ("dS", "dG"):
                p.wait("gpsimd", q, p.val(q))
            for sem in ("V", "A", "G", "T"):
                p.wait("gpsimd", sem, p.val(sem))
            p.do("gpsimd", lambda e: e.dma_start(
                out=ed[:, :], in_=XFFM[:]), inc="dG", by=16)
            c2 = [0]

            def edump(src, n):
                c0 = c2[0]
                c2[0] += n
                p.do("gpsimd", lambda e, src=src, c0=c0, n=n: e.dma_start(
                    out=ed2[:, c0:c0 + n], in_=src[:],
                    allow_slow_non_contiguous=True), inc="dG", by=16)
            for t_ in range(NT):
                edump(HWT[t_], 128)
            edump(WFN, 8)
            edump(ADP, 8)
            edump(AD2, 8)
            edump(GEX, 8)
            edump(THR, 1)
            edump(GIV, 1)
            edump(WIV, 1)
            edump(WSM, 1)
            edump(HWB, 128)
            ot3 = p.val("dG")
            p.wait("gpsimd", "dG", ot3)
            p.stopped = True

        wbt_tok = {0: 0, 1: 0}
        wbt_pe_done = {0: 0, 1: 0}
        p.wait("gpsimd", "V", pre_v_last)
        p.wait("gpsimd", "A", hwt_tok[NT - 1])
        p.wait("gpsimd", "dS", x2_wr[NT - 1])
        wbt_tok[0] = d_gp(lambda e: e.dma_start(
            out=WBT[0][:], in_=wbth[0:128, :]))
        evac_e = {0: 0, 1: 0, 2: 0}
        dma_e = {0: 0, 1: 0, 2: 0}
        ffn_wr = 0
        main_pe_last = 0
        for oc in range(8):
            wp_ = oc % 2
            if oc + 1 < 8:
                p.wait("gpsimd", "T", wbt_pe_done[(oc + 1) % 2])
                wbt_tok[(oc + 1) % 2] = d_gp(lambda e, oc=oc: e.dma_start(
                    out=WBT[(oc + 1) % 2][:],
                    in_=wbth[128 * (oc + 1):128 * (oc + 2), :]))
            p.wait("tensor", "dG", wbt_tok[wp_])
            if oc == 0:
                p.wait("tensor", "dG", tok_e2)
                p.wait("tensor", "A", bc_ev)
            for t in range(NT):
                i = oc * NT + t
                sl = i % 3
                p.wait("tensor", "V", evac_e[sl])
                if oc == 0:
                    p.wait("tensor", "A", hwt_tok[t])
                for kc in range(NKC):
                    p.do("tensor", lambda e, kc=kc, t=t, sl=sl, wp_=wp_,
                         first=(kc == 0): e.matmul(
                        BPS[sl][:, :],
                        XFFM[:, 1024 * kc + 128 * t:
                                 1024 * kc + 128 * (t + 1)],
                        WBT[wp_][:, 512 * kc:512 * (kc + 1)],
                        start=first, stop=False))
                tk = p.do("tensor", lambda e, t=t, sl=sl, oc=oc: e.matmul(
                    BPS[sl][:, :], HWT[t][:],
                    BPKB[:, 512 * oc:512 * (oc + 1)], start=False,
                    stop=True), inc="T")
                wbt_pe_done[wp_] = tk
                main_pe_last = tk
                p.wait("vector", "T", tk)
                p.wait("vector", "dS", dma_e[sl])
                p.do("vector", lambda e, sl=sl, oc=oc: e.tensor_tensor(
                    out=FFNE[sl][:], in0=BPS[sl][:],
                    in1=BBBC[:, 512 * oc:512 * (oc + 1)], op=ALU.add))
                evac_e[sl] = fence("vector")
                p.wait("sync", "V", evac_e[sl])
                dma_e[sl] = d_sync(lambda e, t=t, oc=oc, sl=sl: e.dma_start(
                    out=ffnh[128 * t:128 * (t + 1),
                             512 * oc:512 * (oc + 1)],
                    in_=FFNE[sl][:]))
                ffn_wr = dma_e[sl]

        out_tok = 0
        for t in range(NT):
            p.wait("sync", "dS", ffn_wr)
            p.wait("sync", "T", main_pe_last)
            d_sync(lambda e, t=t: e.dma_start(
                out=YTB[:], in_=ffnh[128 * t:128 * (t + 1), :]))
            lt2 = d_sync(lambda e, t=t: e.dma_start(
                out=XTB[:], in_=x2h[128 * t:128 * (t + 1), :]))
            p.wait("vector", "dS", lt2)
            for i in range(8):
                p.do("vector", lambda e, i=i: e.bn_stats(
                    out=BNS[:, 6 * i:6 * (i + 1)],
                    in_=YTB[:, 512 * i:512 * (i + 1)]))
            p.do("vector", lambda e: e.bn_aggr(
                out=MV[:], in_=BNS[:]))
            tok_mv4 = fence("vector")
            p.wait("scalar", "V", tok_mv4)
            p.do("scalar", lambda e: e.activation(
                out=MCOL[:], in_=MV[:, 1:2], func=AF.Ln, bias=EPSC[:],
                scale=1.0))
            afw()
            p.do("scalar", lambda e: e.activation(
                out=RCOL[:], in_=MCOL[:], func=AF.Exp, scale=-0.5))
            tok_r4 = fence("scalar")
            p.wait("vector", "A", tok_r4)
            p.wait("vector", "V", tok_mv4)
            p.wait("vector", "O", out_tok)
            p.do("vector", lambda e: e.tensor_scalar(
                out=X2B[:], in0=YTB[:], scalar1=MV[:, 0:1], scalar2=RCOL[:],
                op0=ALU.subtract, op1=ALU.mult))
            p.do("vector", lambda e: e.tensor_tensor(
                out=X2B[:], in0=X2B[:], in1=XTB[:], op=ALU.add))
            tok_o = fence("vector")
            p.wait("sync", "V", tok_o)
            out_tok = p.do("sync", lambda e, t=t: e.dma_start(
                out=y[128 * t:128 * (t + 1), :], in_=X2B[:]), inc="O",
                by=16)

        if DBG_DUMP:
            p.stopped = False
            for q in ("dS", "dG"):
                p.wait("sync", q, p.val(q))
            for sem in ("V", "A", "G", "T"):
                p.wait("sync", sem, p.val(sem))
            for nm, src in (("xfm", xfm), ("qfm", qfm), ("aofm", aofm),
                            ("pp", pp), ("ppo", ppo), ("x2h", x2h),
                            ("ffnh", ffnh)):
                out_tok = p.do("sync", lambda e, nm=nm, src=src: e.dma_start(
                    out=dumps[nm][:, :], in_=src[:, :]), inc="O", by=16)
        p.wait("sync", "O", out_tok)
        psE.close()

        block = ctx.enter_context(nc.Block())

        @block.sync
        def _(e):
            for f in p.ops["sync"]:
                f(e)

        @block.gpsimd
        def _(e):
            for f in p.ops["gpsimd"]:
                f(e)

        @block.vector
        def _(e):
            for f in p.ops["vector"]:
                f(e)

        @block.scalar
        def _(e):
            for f in p.ops["scalar"]:
                f(e)

        @block.tensor
        def _(e):
            for f in p.ops["tensor"]:
                f(e)

    return nc


def _alibi_slopes(n_heads=32):
    cp2 = 2 ** math.floor(math.log2(n_heads))
    base = 2 ** (-(2 ** (-(math.log2(cp2) - 3))))
    slopes = base ** np.arange(1, cp2 + 1, dtype=np.float32)
    if cp2 != n_heads:
        extra_base = 2 ** (-(2 ** (-(math.log2(2 * cp2) - 3))))
        n_rem = min(cp2, n_heads - cp2)
        extra = extra_base ** np.arange(1, 1 + 2 * n_rem, 2, dtype=np.float32)
        slopes = np.concatenate([slopes, extra])
    return slopes.astype(np.float32)


def kernel(**inputs):
    x = np.asarray(inputs["L0_attn_hidden_states"], dtype=np.float32)
    w_attn = np.asarray(inputs["L0_attn_w_attn"], dtype=np.float32)
    b_attn = np.asarray(inputs["L0_attn_b_attn"], dtype=np.float32)
    w_proj = np.asarray(inputs["L0_attn_w_proj"], dtype=np.float32)
    b_proj = np.asarray(inputs["L0_attn_b_proj"], dtype=np.float32)
    W_base = np.asarray(inputs["L0_ffn_W_base"], dtype=np.float32)
    b_base = np.asarray(inputs["L0_ffn_b_base"], dtype=np.float32)
    W_gate = np.asarray(inputs["L0_ffn_W_gate"], dtype=np.float32)
    W_thr = np.asarray(inputs["L0_ffn_W_thr"], dtype=np.float32)
    b_thr = np.asarray(inputs["L0_ffn_b_thr"], dtype=np.float32)
    A = np.asarray(inputs["L0_ffn_A"], dtype=np.float32)
    Bm = np.asarray(inputs["L0_ffn_B"], dtype=np.float32)

    slopes = _alibi_slopes(32)
    iotak = (np.arange(NKC, dtype=np.float32)[None, :] * 128
             + np.arange(128, dtype=np.float32)[:, None]).astype(np.float32)
    iotak2 = np.concatenate(
        [iotak - 512.0 * s_ for s_ in range(NQS)], axis=1).astype(np.float32)
    iota512 = np.ascontiguousarray(
        np.arange(TQS, dtype=np.float32)[None, :])
    maskpos = np.where(np.arange(128)[:, None] > np.arange(128)[None, :],
                       np.float32(BIG), np.float32(0.0)).astype(np.float32)
    identf = np.eye(128, dtype=np.float32)

    w9 = np.concatenate([W_gate.T, W_thr.T], axis=1)
    w9h = np.ascontiguousarray(
        w9.reshape(32, 128, 9).transpose(1, 0, 2)).reshape(128, 288)
    apackh = np.ascontiguousarray(
        A.reshape(128, 4096).T.reshape(32, 128, 128).transpose(1, 0, 2)
    ).reshape(128, 4096)
    bpackh = np.ascontiguousarray(Bm.transpose(0, 2, 1).reshape(128, 4096))
    wbth = np.ascontiguousarray(
        W_base.reshape(8, 512, 32, 128).transpose(0, 3, 2, 1)
    ).reshape(1024, 16384)
    bthr_rep = np.ascontiguousarray(
        np.broadcast_to(b_thr.reshape(1, 1), (128, 1))).astype(np.float32)

    def wh(W):
        return np.ascontiguousarray(
            W.reshape(32, 128, 2, 512).transpose(2, 1, 0, 3)
        ).reshape(256, 16384)

    nc = build_nc()
    in_maps = []
    for c in range(8):
        b = c // 4
        g = c % 4
        xb = x[b]
        xts = np.ascontiguousarray(
            xb.reshape(16, 256, 32, 128).transpose(0, 3, 2, 1)
        ).reshape(2048, 8192)
        xown = np.ascontiguousarray(xb[1024 * g:1024 * (g + 1)])
        WQ = w_attn[:, 1024 * g:1024 * (g + 1)]
        WK = w_attn[:, 4096 + 1024 * g:4096 + 1024 * (g + 1)]
        WV = w_attn[:, 8192 + 1024 * g:8192 + 1024 * (g + 1)]
        bq = b_attn[1024 * g:1024 * (g + 1)]
        bk = b_attn[4096 + 1024 * g:4096 + 1024 * (g + 1)]
        bv = b_attn[8192 + 1024 * g:8192 + 1024 * (g + 1)]
        bqkc = np.ascontiguousarray(np.concatenate(
            [bq.reshape(8, 128).T, bk.reshape(8, 128).T], axis=1))
        wphc = np.ascontiguousarray(
            w_proj[1024 * g:1024 * (g + 1)].reshape(8, 128, 4096)
            .transpose(1, 0, 2)).reshape(128, 32768)
        sl = slopes[8 * g:8 * (g + 1)]
        in_maps.append({
            "xts": xts,
            "xown": xown,
            "wqh": wh(WQ), "wkh": wh(WK), "wvh": wh(WV),
            "bqk": bqkc,
            "bvrow": np.ascontiguousarray(bv.reshape(1, 1024)),
            "wph": wphc,
            "bprow": np.ascontiguousarray(b_proj.reshape(1, 4096)),
            "wbth": wbth,
            "bbrow": np.ascontiguousarray(b_base.reshape(1, 4096)),
            "w9h": w9h,
            "apackh": apackh,
            "bpackh": bpackh,
            "bthr": bthr_rep,
            "maskpos": maskpos,
            "iotak": iotak,
            "iotak2": iotak2,
            "iota512": iota512,
            "slopesc": np.ascontiguousarray(
                np.broadcast_to(sl[None, :], (128, 8))).astype(np.float32),
            "slopesr": np.ascontiguousarray(sl.reshape(1, 8)),
            "identf": identf,
        })

    res = run_bass_kernel_spmd(nc, in_maps, core_ids=list(range(8)))
    out = np.empty((2, 4096, 4096), dtype=np.float32)
    for c in range(8):
        b = c // 4
        g = c % 4
        out[b, 1024 * g:1024 * (g + 1), :] = res.results[c]["y"]
    return out
